# version 12 (final)
# speedup vs baseline: 1.8617x; 1.0212x over previous
import math
import numpy as np
import concourse.bass as bass
import concourse.mybir as mybir
from concourse.bass_utils import run_bass_kernel_spmd

F32 = mybir.dt.float32
BF16 = mybir.dt.bfloat16
AF = mybir.ActivationFunctionType
ALU = mybir.AluOpType
AX = mybir.AxisListType

T = 2048
NSEQ = 2
NH = 4
EPS = 1e-5

import os
DEBUG_TAPS = os.environ.get("K_TAPS", "").split(",") if os.environ.get("K_TAPS") else []
MAX_PHASE = int(os.environ.get("K_MAX_PHASE", "4"))
MAX_SEQ = int(os.environ.get("K_MAX_SEQ", "2"))
P1_PARTS = int(os.environ.get("K_P1_PARTS", "9"))
GLA_PARTS = int(os.environ.get("K_GLA_PARTS", "9"))
PARANOID = int(os.environ.get("K_PARANOID", "0"))
N_HEADS = int(os.environ.get("K_HEADS", "4"))
N_CHUNKS = int(os.environ.get("K_CHUNKS", "8"))
N_HALVES = int(os.environ.get("K_HALVES", "2"))


class _Op:
    __slots__ = ("eng", "fn", "deps", "kind", "need_inc", "inc_val", "inc_sem", "waits", "idx", "r", "w", "slot")


class Em:
    def __init__(self):
        self.ops = []
        self.writer = {}
        self.readers = {}
        self.paranoid = False

    def op(self, eng, fn, r=(), w=(), kind="cmp", deps=(), slot=None):
        i = len(self.ops)
        _dbg_r, _dbg_w = list(r), list(w)
        dd = set(deps)
        for k in r:
            j = self.writer.get(k)
            if j is not None:
                dd.add(j)
        for k in w:
            j = self.writer.get(k)
            if j is not None:
                dd.add(j)
            dd.update(self.readers.get(k, ()))
        o = _Op()
        o.r, o.w = _dbg_r, _dbg_w
        o.slot = slot
        o.eng, o.fn, o.kind, o.idx = eng, fn, kind, i
        o.need_inc = kind == "dma"
        o.inc_val = None
        o.inc_sem = None
        o.deps = set()
        for j in dd:
            if j == i:
                continue
            p = self.ops[j]
            if p.kind == "cmp" and p.eng == eng:
                continue
            o.deps.add(j)
        self.ops.append(o)
        for k in r:
            self.readers.setdefault(k, set()).add(i)
        for k in w:
            self.writer[k] = i
            self.readers[k] = set()
        return i

    def last_per_engine(self):
        out = {}
        for o in self.ops:
            out[o.eng] = o.idx
        return out

    def finalize(self, paranoid=False):
        for o in self.ops:
            if paranoid:
                o.need_inc = True
            for j in o.deps:
                self.ops[j].need_inc = True
        cnt = {"T": 0, "S": 0, "V": 0, "G": 0}
        for o in self.ops:
            if o.kind == "dma":
                sl = o.slot or "DOUT"
                cnt[sl] = cnt.get(sl, 0) + 16
                o.inc_val = cnt[sl]
                o.inc_sem = sl
            elif o.need_inc:
                cnt[o.eng] += 1
                o.inc_val = cnt[o.eng]
                o.inc_sem = o.eng
        wm = {}
        for o in self.ops:
            waits = {}
            for j in o.deps:
                p = self.ops[j]
                if p.inc_val > waits.get(p.inc_sem, 0):
                    waits[p.inc_sem] = p.inc_val
            o.waits = []
            for s, v in waits.items():
                if s == "DC":
                    v = cnt["DC"]
                if not self.paranoid and wm.get((o.eng, s), 0) >= v:
                    continue
                wm[(o.eng, s)] = v
                o.waits.append((s, v))
        self.sem_names = ["T", "S", "V", "G"] + sorted(
            k for k in cnt if k not in ("T", "S", "V", "G"))
        return cnt


def ks(name, lo, hi, g):
    return [(name, x) for x in range(lo // g, (hi + g - 1) // g)]


def build_nc():
    nc = bass.Bass("TRN2", target_bir_lowering=False)

    def din(name, shape):
        return nc.dram_tensor(name, list(shape), F32, kind="ExternalInput").ap()

    x0 = din("x0", (NSEQ, 8, 128, T))
    wqk = din("wqk", (NH, 128, 4096))
    wktm = din("wktm", (NH, 128, 2048))
    wvtm = din("wvtm", (NH, 128, 4096))
    wgtm = din("wgtm", (NH, 128, 4096))
    wgk1 = din("wgk1", (128, 128))
    wgk2b = din("wgk2b", (17, 1024))
    gnw = din("gnw", (1, 512))
    wof = din("wof", (16, 128, 1024))
    w1t = din("w1t", (32, 128, 1024))
    b1c = din("b1c", (128, 32))
    w2t = din("w2t", (32, 128, 1024))
    b2c = din("b2c", (128, 8))
    wqkv = din("wqkv", (24, 128, 1024))
    bqkv = din("bqkv", (128, 24))
    wrkv = din("wrkv", (3, 8, 128, 1024))
    mur = din("mur", (128, 8))
    muk = din("muk", (128, 8))
    scl = din("scl", (16, 1))
    cst = din("cst", (128, 3 * 128))
    onesd = din("onesd", (128, 1024))
    cvals = din("cvals", (128, 8))
    out_d = nc.dram_tensor("out", [NSEQ, 16, 128, 1024], F32, kind="ExternalOutput").ap()
    taps = {t: nc.dram_tensor("dbg_" + t, [NSEQ, 8, 128, T], F32, kind="ExternalOutput").ap()
            for t in DEBUG_TAPS}
    TAP2 = int(os.environ.get("K_TAP2", "0"))
    if TAP2:
        dbg_vc = nc.dram_tensor("dbg_vc", [128, 16640], F32, kind="ExternalOutput").ap()
        dbg_xs = nc.dram_tensor("dbg_xs", [128, 4096], F32, kind="ExternalOutput").ap()
        dbg_afs = nc.dram_tensor("dbg_afs", [128, 2824], F32, kind="ExternalOutput").ap()
        dbg_sfp = nc.dram_tensor("dbg_sfp", [128, 1024], F32, kind="ExternalOutput").ap()

    import contextlib
    es = contextlib.ExitStack()

    with es:
        def sba(name, p, f, dt):
            return es.enter_context(nc.sbuf_tensor(name, [p, f], dt))

        ST = sba("ST", 128, 16384, F32)
        XS = sba("XS", 128, 4096, BF16)
        A1 = sba("A1", 128, 16384, BF16)
        VC = sba("VC", 128, 16640, BF16)
        XH = sba("XH", 128, 8192, BF16)
        AFS = sba("AFS", 128, 2824, F32)
        SFP = sba("SFP", 128, 1024, F32)
        SBFT = sba("SBFT", 128, 1024, BF16)
        SQT = sba("SQT", 128, 1024, BF16)
        SGQK = sba("SGQK", 128, 4096, BF16)
        SGKV = sba("SGKV", 128, 4096, BF16)
        SGSM = sba("SGSM", 128, 4096, BF16)
        SROW = sba("SROW", 1, 1024, F32)
        LTB = sba("LTB", 128, 128, BF16)
        LOW = sba("LOW", 17, 1024, BF16)
        WG1 = sba("WG1", 128, 128, BF16)
        WG2 = sba("WG2", 17, 1024, BF16)
        GNB = sba("GNB", 128, 512, BF16)
        BQK = sba("BQK", 128, 24, F32)
        B1S = sba("B1S", 128, 32, F32)
        B2S = sba("B2S", 128, 8, F32)
        MURS = sba("MURS", 128, 8, F32)
        MUKS = sba("MUKS", 128, 8, F32)
        OMR = sba("OMR", 128, 8, F32)
        OMK = sba("OMK", 128, 8, F32)
        SC16 = sba("SC16", 16, 1, F32)
        IDF = sba("IDF", 128, 128, F32)
        IDB = sba("IDB", 128, 128, BF16)
        TRB = sba("TRB", 128, 128, BF16)
        MSB = sba("MSB", 128, 128, BF16)
        ON1F = sba("ON1F", 1, 128, F32)
        ONCB = sba("ONCB", 128, 1, BF16)
        ONVB = sba("ONVB", 128, 64, BF16)
        CVS = sba("CVS", 128, 8, F32)

        PA = es.enter_context(nc.psum_tensor("PA", [128, 1536], F32))
        PB = es.enter_context(nc.psum_tensor("PB", [128, 1536], F32))
        PTF = es.enter_context(nc.psum_tensor("PTF", [128, 512], F32))
        PTB = es.enter_context(nc.psum_tensor("PTB", [128, 1024], BF16))

        em = Em()
        out_dmas = []
        cur_q = [0]

        PSR = [("PA", 0), ("PA", 1), ("PA", 2), ("PB", 0), ("PB", 1), ("PB", 2)]

        def psum_ap(key, lo=0, hi=512):
            t_, r_ = key
            tt_ = PA if t_ == "PA" else PB
            return tt_[:, r_ * 512 + lo: r_ * 512 + hi]

        def ptfk(lo, hi):
            return [("PTF", 0)]

        def barrier():
            last = em.last_per_engine()
            for e_ in ("T", "S", "V", "G"):
                em.op(e_, lambda e: e.nop(),
                      deps=[v for k, v in last.items() if k != e_])

        BAR = int(os.environ.get("K_BAR", "0"))

        def mm(dst, lh, rh, st, sp, r, w):
            em.op("T", lambda e, a=dst, b=lh, c=rh, s=st, p=sp:
                  e.matmul(a, b, c, start=s, stop=p), r=r, w=w)

        def transpose(dst, src, ident, r, w):
            em.op("T", lambda e, a=dst, b=src, c=ident: e.transpose(a, b, c), r=r, w=w)

        def act(dst, src, func, r, w, bias=0.0, scale=1.0):
            em.op("S", lambda e, a=dst, b=src, f=func, bi=bias, sc=scale:
                  e.activation(a, b, f, bias=bi, scale=sc), r=r, w=w)

        def scopy(dst, src, r, w):
            em.op("S", lambda e, a=dst, b=src: e.copy(a, b), r=r, w=w)

        def tt(dst, a_, b_, op, r, w, eng="V"):
            em.op(eng, lambda e, o=dst, a=a_, b=b_, p=op: e.tensor_tensor(o, a, b, p), r=r, w=w)

        def tsmul(dst, a_, s_, r, w):
            em.op("V", lambda e, o=dst, a=a_, s=s_: e.tensor_scalar_mul(o, a, s), r=r, w=w)

        def stt(dst, a_, s_, b_, op0, op1, r, w):
            em.op("V", lambda e, o=dst, a=a_, s=s_, b=b_, p0=op0, p1=op1:
                  e.scalar_tensor_tensor(o, a, s, b, p0, p1), r=r, w=w)

        def vcopy(dst, src, r, w):
            em.op("V", lambda e, a=dst, b=src: e.tensor_copy(a, b), r=r, w=w)

        def vred(dst, src, r, w):
            em.op("V", lambda e, a=dst, b=src: e.reduce_sum(a, b, AX.X), r=r, w=w)

        def memset(dst, val, r, w):
            em.op("V", lambda e, a=dst, v=val: e.memset(a, v), r=r, w=w)

        def dma(q, dst, src, r, w, slow=False, slot="DOUT"):
            if slow:
                return em.op(q, lambda e, a=dst, b=src: e.dma_start(
                    out=a, in_=b, allow_slow_non_contiguous=True),
                    r=r, w=w, kind="dma", slot=slot)
            return em.op(q, lambda e, a=dst, b=src: e.dma_start(out=a, in_=b),
                         r=r, w=w, kind="dma", slot=slot)

        def stk(c, s):
            return [("ST", c, s)]

        def xsk(lo, hi):
            return ks("XS", lo, hi, 128)

        def vck(lo, hi):
            return ks("VC", lo, hi, 256)

        def a1k(lo, hi):
            return ks("A1", lo, hi, 512)

        def xhk(lo, hi):
            return ks("XH", lo, hi, 512)

        def afk(lo, hi):
            return ks("AF", lo, hi, 128)

        def qkk(lo, hi):
            return ks("QKS", lo, hi, 512)

        def kvk(lo, hi):
            return ks("KVS", lo, hi, 512)

        dma("Y", IDF[:], cst[:, 0:128], (), [("IDF", 0)], slot="DC")
        dma("G", IDB[:], cst[:, 0:128], (), [("IDB", 0)], slot="DCG")
        dma("G", TRB[:], cst[:, 128:256], (), [("TRB", 0)], slot="DCG")
        dma("G", MSB[:], cst[:, 256:384], (), [("MSB", 0)], slot="DCG")
        dma("Y", ON1F[:], onesd[0:1, 0:128], (), [("ON1F", 0)], slot="DC")
        dma("G", ONCB[:], onesd[:, 0:1], (), [("ONCB", 0)], slow=True, slot="DCG")
        dma("G", ONVB[:], onesd[:, 0:64], (), [("ONVB", 0)], slot="DCG")
        dma("G", WG1[:], wgk1, (), [("WG1", 0)], slot="DCG")
        dma("G", WG2[0:17, :], wgk2b, (), [("WG2", 0)], slot="DCG")
        dma("Y", BQK[:], bqkv, (), [("BQK", 0)], slot="DC")
        dma("Y", B1S[:], b1c, (), [("B1S", 0)], slot="DC")
        dma("Y", B2S[:], b2c, (), [("B2S", 0)], slot="DC")
        dma("Y", MURS[:], mur, (), [("MUR", 0)], slot="DC")
        dma("Y", MUKS[:], muk, (), [("MUK", 0)], slot="DC")
        dma("Y", SC16[:], scl, (), [("SC16", 0)], slow=True, slot="DC")
        dma("Y", CVS[:], cvals, (), [("CVS", 0)], slot="DC")
        dma("Y", SROW[0:1, 0:512], gnw, (), [("SROW", 0)], slot="DC")
        mm(PTF[:, 0:512], ON1F[:], SROW[0:1, 0:512], True, True,
           [("ON1F", 0), ("SROW", 0)], ptfk(0, 512))
        vcopy(GNB[:], PTF[:, 0:512], ptfk(0, 512), [("GNB", 0)])
        em.op("V", lambda e: e.tensor_scalar(OMR[:], MURS[:], -1.0, 1.0, ALU.mult, ALU.add),
              r=[("MUR", 0)], w=[("OMR", 0)])
        em.op("V", lambda e: e.tensor_scalar(OMK[:], MUKS[:], -1.0, 1.0, ALU.mult, ALU.add),
              r=[("MUK", 0)], w=[("OMK", 0)])
        tsmul(BQK[:, 0:8], BQK[:, 0:8], 0.125, [("BQK", 0)], [("BQK", 0)])
        memset(PA[:, 0:1024], 0.0, (), [("PA", 0), ("PA", 1)])
        memset(PB[:, 512:1536], 0.0, (), [("PB", 1), ("PB", 2)])

        C_EB, C_ENB, C_EBLB, C_S1, C_S2 = 0, 256, 512, 768, 1280
        C_SQ, C_ON = 768, 1280
        C_MX, C_YT = 0, 512
        C_RB = 0
        C_RSUM, C_LNC, C_RS = 1792, 1793, 1794
        C_X0R = 1800

        def afa(lo, hi):
            return AFS[:, lo:hi]

        def st_ap(c, lo, hi):
            return ST[:, c * 2048 + lo: c * 2048 + hi]

        def xs_ap(c):
            return XS[:, c * 512:(c + 1) * 512]

        def layer_norm(s, tag):
            ss = s * 512
            for c in range(8):
                scopy(xs_ap(c), st_ap(c, ss, ss + 512), stk(c, s), xsk(c * 512, (c + 1) * 512))
            for c in range(8):
                mm(PA[0:1, 0:512], ONCB[:], xs_ap(c), c == 0, c == 7,
                   xsk(c * 512, (c + 1) * 512) + [("ONCB", 0)], [("PA", 0)])
            for c in range(8):
                sl_ = (c % 2) * 512
                act(SQT[:, sl_:sl_ + 512], xs_ap(c), AF.Square,
                    xsk(c * 512, (c + 1) * 512), [("SQT", c % 2)])
                mm(PA[0:1, 512:1024], ONCB[:], SQT[:, sl_:sl_ + 512], c == 0, c == 7,
                   [("SQT", c % 2), ("ONCB", 0)], [("PA", 1)])
            tsmul(SROW[0:1, 0:512], PA[0:1, 0:512], 1.0 / 1024.0, [("PA", 0)], [("SROW", 0)])
            tt(SROW[0:1, 512:1024], SROW[0:1, 0:512], SROW[0:1, 0:512], ALU.mult,
               [("SROW", 0)], [("SROW", 1)])
            stt(SROW[0:1, 512:1024], PA[0:1, 512:1024], 1.0 / 1024.0, SROW[0:1, 512:1024],
                ALU.mult, ALU.subtract, [("PA", 1), ("SROW", 1)], [("SROW", 1)])
            act(SROW[0:1, 512:1024], SROW[0:1, 512:1024], AF.Ln,
                [("SROW", 1), ("CVS", 0)], [("SROW", 1)], bias=CVS[0:1, 2:3])
            act(SROW[0:1, 512:1024], SROW[0:1, 512:1024], AF.Exp,
                [("SROW", 1), ("CVS", 0)], [("SROW", 1)], scale=CVS[0:1, 3:4])
            mm(PB[:, 0:512], ON1F[:], SROW[0:1, 512:1024], True, True,
               [("SROW", 1), ("ON1F", 0)], [("PB", 0)])
            tt(SROW[0:1, 0:512], SROW[0:1, 0:512], SROW[0:1, 512:1024], ALU.mult,
               [("SROW", 0), ("SROW", 1)], [("SROW", 0)])
            mm(PB[:, 512:1024], ON1F[:], SROW[0:1, 0:512], True, True,
               [("SROW", 0), ("ON1F", 0)], [("PB", 1)])
            for c in range(8):
                tt(st_ap(c, ss, ss + 512), st_ap(c, ss, ss + 512), PB[:, 0:512], ALU.mult,
                   stk(c, s) + [("PB", 0)], stk(c, s))
                tt(st_ap(c, ss, ss + 512), st_ap(c, ss, ss + 512), PB[:, 512:1024],
                   ALU.subtract, stk(c, s) + [("PB", 1)], stk(c, s))
            if tag in taps:
                for c in range(8):
                    i_ = dma("Y", taps[tag][cur_q[0], c, :, ss:ss + 512],
                             st_ap(c, ss, ss + 512), stk(c, s), ())
                    out_dmas.append(i_)

        def build_p1(q):
            VQ, VK, VKT, VGK, VV, VGG = 0, 2048, 4096, 6144, 8192, 12288
            XQG, XKI, XKS, XP, XOG = 0, 256, 512, 768, 1024
            sm_slot = [0]

            for h in range(NH if P1_PARTS >= 1 else 0):
                for half in range(N_HALVES):
                    hb = half * 1024
                    for c in range(8):
                        dma("G", XH[:, c * 1024:(c + 1) * 1024], x0[q, c, :, hb:hb + 1024],
                            (), xhk(c * 1024, (c + 1) * 1024), slot="XH%d" % c)
                    for hs in range(2):
                        for c in range(8):
                            mm(PTF[0:16, 0:512], WG1[:, c * 16:(c + 1) * 16],
                               XH[:, c * 1024 + hs * 512: c * 1024 + hs * 512 + 512],
                               c == 0, c == 7,
                               [("WG1", 0)] + xhk(c * 1024 + hs * 512, c * 1024 + hs * 512 + 512),
                               ptfk(0, 512))
                        scopy(LOW[0:16, hs * 512:hs * 512 + 512], PTF[0:16, 0:512],
                              ptfk(0, 512), [("LOW", hs)])
                    dma("G", LOW[16:17, 0:1024], onesd[0:1, 0:1024], (), [("LOW", 2)], slot="LOWO")

                    if P1_PARTS >= 6:
                        for oc_ in range(8):
                            wb_ = 4096 + oc_ * 1024
                            dma("G", A1[:, wb_:wb_ + 1024], wof[oc_ * 2 + h // 2],
                                (), a1k(wb_, wb_ + 1024), slot="WO%d" % oc_)
                    dma("G", SGQK[:, 0:2048], wqk[h, :, 0:2048],
                        (), qkk(0, 2048), slot="QKa")
                    dma("G", SGQK[:, 2048:4096], wqk[h, :, 2048:4096],
                        (), qkk(2048, 4096), slot="QKb")
                    for j in range(4):
                        for hs in range(2):
                            reg = PSR[(j * 2 + hs) % 4]
                            for c in range(8):
                                mm(psum_ap(reg),
                                   SGQK[:, c * 512 + j * 128: c * 512 + j * 128 + 128],
                                   XH[:, c * 1024 + hs * 512: c * 1024 + hs * 512 + 512],
                                   c == 0, c == 7,
                                   qkk(c * 512, (c + 1) * 512) +
                                   xhk(c * 1024 + hs * 512, c * 1024 + hs * 512 + 512),
                                   [reg])
                            base = (VQ if j < 2 else VK) + (j % 2) * 1024 + hs * 512
                            if j < 2:
                                act(VC[:, base:base + 512], psum_ap(reg), AF.Identity,
                                    [reg, ("CVS", 0)], vck(base, base + 512),
                                    scale=CVS[:, 4:5])
                            else:
                                act(VC[:, base:base + 512], psum_ap(reg), AF.Identity,
                                    [reg], vck(base, base + 512))
                    for cg, (wt, width, dstb) in enumerate(
                            [(wktm, 256, VKT), (wvtm, 512, VV), (wgtm, 512, VGG)]):
                        nsl = 8 * width
                        for half2 in range(nsl // 2048):
                            dma("G", SGKV[:, half2 * 2048:(half2 + 1) * 2048],
                                wt[h, :, half2 * 2048:(half2 + 1) * 2048],
                                (), kvk(half2 * 2048, (half2 + 1) * 2048),
                                slot="KV%d" % half2)
                        for tb in range(8):
                            reg = PSR[tb % 2]
                            for c in range(8):
                                mm(psum_ap(reg, 0, width),
                                   XH[:, c * 1024 + tb * 128: c * 1024 + tb * 128 + 128],
                                   SGKV[:, c * width: (c + 1) * width],
                                   c == 0, c == 7,
                                   kvk(c * width, (c + 1) * width) +
                                   xhk(c * 1024 + tb * 128, c * 1024 + tb * 128 + 128),
                                   [reg])
                            base = dstb + tb * width
                            if cg == 0:
                                vcopy(VC[:, base:base + width], psum_ap(reg, 0, width),
                                      [reg], vck(base, base + width))
                            elif cg == 1:
                                scopy(VC[:, base:base + width], psum_ap(reg, 0, width),
                                      [reg], vck(base, base + width))
                            else:
                                act(afa(C_S1, C_S1 + 512), psum_ap(reg), AF.Exp,
                                    [reg], afk(C_S1, C_S1 + 512), scale=-1.0)
                                act(afa(C_S1, C_S1 + 512), afa(C_S1, C_S1 + 512), AF.Ln,
                                    afk(C_S1, C_S1 + 512), afk(C_S1, C_S1 + 512), bias=1.0)
                                act(afa(C_S2, C_S2 + 512), afa(C_S1, C_S1 + 512), AF.Exp,
                                    afk(C_S1, C_S1 + 512), afk(C_S2, C_S2 + 512), scale=-1.0)
                                tt(afa(C_S1, C_S1 + 512), psum_ap(reg), afa(C_S2, C_S2 + 512),
                                   ALU.mult, [reg] + afk(C_S2, C_S2 + 512),
                                   afk(C_S1, C_S1 + 512))
                                tt(VC[:, base:base + 512], afa(C_S1, C_S1 + 512), GNB[:],
                                   ALU.mult, afk(C_S1, C_S1 + 512) + [("GNB", 0)],
                                   vck(base, base + 512))
                    for tb in range(8 if P1_PARTS >= 4 else 0):
                        mm(PTF[:, 0:256], LOW[0:17, tb * 128:(tb + 1) * 128],
                           WG2[0:17, h * 256:(h + 1) * 256], True, True,
                           [("LOW", tb // 4), ("LOW", 2), ("WG2", 0)], ptfk(0, 256))
                        act(afa(C_S1, C_S1 + 256), PTF[:, 0:256], AF.Exp,
                            ptfk(0, 256), afk(C_S1, C_S1 + 256), scale=-1.0)
                        act(afa(C_S1, C_S1 + 256), afa(C_S1, C_S1 + 256), AF.Ln,
                            afk(C_S1, C_S1 + 256), afk(C_S1, C_S1 + 256), bias=1.0)
                        base = VGK + tb * 256
                        tsmul(VC[:, base:base + 256], afa(C_S1, C_S1 + 256), -1.0 / 16.0,
                              afk(C_S1, C_S1 + 256), vck(base, base + 256))

                    def xpar(par, off, w):
                        b = par * 1664 + off
                        return XS[:, b:b + w], xsk(b, b + w)

                    OGB = 3328

                    def stage_a(cc):
                        par = cc % 2
                        gkb = VGK + cc * 256
                        if par == 1:
                            cps, cpk0, cpk1 = PA[:, 1024:1536], [("PA", 2)], [("PA", 2)]
                        else:
                            cps, cpk0, cpk1 = PTF[:, 0:512], ptfk(0, 256), ptfk(256, 512)
                        for sl in range(2):
                            mm(cps[:, sl * 128:(sl + 1) * 128],
                               VC[:, gkb + sl * 128: gkb + sl * 128 + 128], TRB[:],
                               True, True, vck(gkb, gkb + 256) + [("TRB", 0)], cpk0)
                        mm(cps[:, 256:512], MSB[:], VC[:, gkb:gkb + 256], True, True,
                           vck(gkb, gkb + 256) + [("MSB", 0)], cpk1)
                        eb, ebk = xpar(par, 896, 256)
                        enb, enbk = xpar(par, 1152, 256)
                        eblb, eblbk = xpar(par, 1408, 256)
                        act(eb, cps[:, 0:256], AF.Exp, cpk0, ebk)
                        act(enb, cps[:, 0:256], AF.Exp, cpk0, enbk, scale=-1.0)
                        act(eblb, cps[:, 256:512], AF.Exp, cpk1, eblbk)
                        eclc = 1795 + par
                        for sl in range(2):
                            act(AFS[:, eclc + 2 * sl: eclc + 2 * sl + 1],
                                cps[:, sl * 128 + 127: sl * 128 + 128], AF.Exp,
                                cpk0, afk(eclc + 2 * sl, eclc + 2 * sl + 1))
                        qg, qgk = xpar(par, 0, 256)
                        ki, kik = xpar(par, 256, 256)
                        kst, kstk = xpar(par, 512, 256)
                        for sl in range(2):
                            qb = VQ + sl * 1024 + cc * 128
                            tt(qg[:, sl * 128:(sl + 1) * 128], VC[:, qb:qb + 128],
                               eb[:, sl * 128:(sl + 1) * 128], ALU.mult,
                               vck(qb, qb + 128) + ebk, qgk)
                            kb_ = VK + sl * 1024 + cc * 128
                            tt(ki[:, sl * 128:(sl + 1) * 128], VC[:, kb_:kb_ + 128],
                               enb[:, sl * 128:(sl + 1) * 128], ALU.mult,
                               vck(kb_, kb_ + 128) + enbk, kik)
                        ktb = VKT + cc * 256
                        tt(kst, VC[:, ktb:ktb + 256], eblb, ALU.mult,
                           vck(ktb, ktb + 256) + eblbk, kstk)

                    def stage_b1(cc):
                        par = cc % 2
                        qg, qgk = xpar(par, 0, 256)
                        ki, kik = xpar(par, 256, 256)
                        p, pk = xpar(par, 768, 128)
                        for sl in range(2):
                            mm(PA[:, 0:128], ki[:, sl * 128:(sl + 1) * 128],
                               qg[:, sl * 128:(sl + 1) * 128], sl == 0, sl == 1,
                               kik + qgk, [("PA", 0)])
                        tt(p, PA[:, 0:128], TRB[:], ALU.mult,
                           [("PA", 0), ("TRB", 0)], pk)

                    def stage_b2(cc):
                        par = cc % 2
                        first = (half == 0 and cc == 0)
                        last = (half == 1 and cc == 7)
                        qg, qgk = xpar(par, 0, 256)
                        kst, kstk = xpar(par, 512, 256)
                        p, pk = xpar(par, 768, 128)
                        vb = VV + cc * 512
                        mm(PA[:, 512:1024], p, VC[:, vb:vb + 512],
                           True, first, pk + vck(vb, vb + 512), [("PA", 1)])
                        if not first:
                            for sl in range(2):
                                mm(PA[:, 512:1024], qg[:, sl * 128:(sl + 1) * 128],
                                   SBFT[:, sl * 512:(sl + 1) * 512], False, sl == 1,
                                   qgk + [("SB", sl)], [("PA", 1)])
                        eclc = 1795 + par
                        if not last:
                            for sl in range(2):
                                mm(psum_ap(PSR[4 + sl]), kst[:, sl * 128:(sl + 1) * 128],
                                   VC[:, vb:vb + 512], True, True,
                                   kstk + vck(vb, vb + 512), [PSR[4 + sl]])
                            for sl in range(2):
                                if first:
                                    vcopy(SFP[:, sl * 512:(sl + 1) * 512],
                                          psum_ap(PSR[4 + sl]), [PSR[4 + sl]], [("SF", sl)])
                                else:
                                    stt(SFP[:, sl * 512:(sl + 1) * 512],
                                        SFP[:, sl * 512:(sl + 1) * 512],
                                        AFS[:, eclc + 2 * sl: eclc + 2 * sl + 1],
                                        psum_ap(PSR[4 + sl]), ALU.mult, ALU.add,
                                        [("SF", sl), PSR[4 + sl]] +
                                        afk(eclc + 2 * sl, eclc + 2 * sl + 1),
                                        [("SF", sl)])
                                scopy(SBFT[:, sl * 512:(sl + 1) * 512],
                                      SFP[:, sl * 512:(sl + 1) * 512],
                                      [("SF", sl)], [("SB", sl)])
                        act(afa(C_SQ, C_SQ + 512), PA[:, 512:1024], AF.Square,
                            [("PA", 1)], afk(C_SQ, C_SQ + 512))
                        vred(afa(C_RSUM, C_RSUM + 1), afa(C_SQ, C_SQ + 512),
                             afk(C_SQ, C_SQ + 512), afk(C_RSUM, C_RSUM + 1))
                        act(afa(C_LNC, C_LNC + 1), afa(C_RSUM, C_RSUM + 1), AF.Ln,
                            afk(C_RSUM, C_RSUM + 1) + [("CVS", 0)], afk(C_LNC, C_LNC + 1),
                            bias=CVS[:, 0:1])
                        act(afa(C_RS, C_RS + 1), afa(C_LNC, C_LNC + 1), AF.Exp,
                            afk(C_LNC, C_LNC + 1) + [("CVS", 0)], afk(C_RS, C_RS + 1),
                            scale=CVS[:, 3:4], bias=CVS[:, 1:2])
                        tsmul(afa(C_ON, C_ON + 512), PA[:, 512:1024], afa(C_RS, C_RS + 1),
                              [("PA", 1)] + afk(C_RS, C_RS + 1), afk(C_ON, C_ON + 512))
                        ggb = VGG + cc * 512
                        og, ogk = XS[:, OGB:OGB + 512], xsk(OGB, OGB + 512)
                        tt(og, afa(C_ON, C_ON + 512), VC[:, ggb:ggb + 512],
                           ALU.mult, afk(C_ON, C_ON + 512) + vck(ggb, ggb + 512), ogk)
                        for s4 in range(4):
                            transpose(PTB[:, s4 * 128:(s4 + 1) * 128],
                                      og[:, s4 * 128:(s4 + 1) * 128], IDB[:],
                                      ogk + [("IDB", 0)], [("PTB", 0)])
                        for s4 in range(4):
                            ob = s4 * 1024 + cc * 128
                            vcopy(A1[:, ob:ob + 128], PTB[:, s4 * 128:(s4 + 1) * 128],
                                  [("PTB", 0)], a1k(ob, ob + 128))

                    if P1_PARTS >= 5:
                        stage_a(0)
                        for cc in range(8):
                            stage_b1(cc)
                            if cc + 1 < 8:
                                stage_a(cc + 1)
                            stage_b2(cc)

                    for hs in range(2 if P1_PARTS >= 6 else 0):
                        s = half * 2 + hs
                        for oc in range(8):
                            if h == 0:
                                x0b = C_X0R + (oc % 2) * 512
                                dma("Y", afa(x0b, x0b + 512), x0[q, oc, :, s * 512:(s + 1) * 512],
                                    (), afk(x0b, x0b + 512), slot="XR%d" % (oc % 2))
                            reg = PSR[oc % 4]
                            wb_ = 4096 + oc * 1024
                            for dc in range(4):
                                hd8 = (h % 2) * 4 + dc
                                ob = dc * 1024 + hs * 512
                                mm(psum_ap(reg), A1[:, wb_ + hd8 * 128:
                                                      wb_ + hd8 * 128 + 128],
                                   A1[:, ob:ob + 512], dc == 0, dc == 3,
                                   a1k(wb_, wb_ + 1024) + a1k(ob, ob + 512), [reg])
                            if h == 0:
                                x0b = C_X0R + (oc % 2) * 512
                                tt(st_ap(oc, s * 512, s * 512 + 512), psum_ap(reg),
                                   afa(x0b, x0b + 512), ALU.add,
                                   [reg] + afk(x0b, x0b + 512), stk(oc, s))
                            else:
                                tt(st_ap(oc, s * 512, s * 512 + 512), psum_ap(reg),
                                   st_ap(oc, s * 512, s * 512 + 512), ALU.add,
                                   [reg] + stk(oc, s), stk(oc, s))
            if P1_PARTS >= 6:
                for s in range(4):
                    layer_norm(s, "x1")

        def build_p2(q):
            sm_slot = [0]
            for c in range(8):
                vcopy(LTB[:, c * 16:(c + 1) * 16], ST[:, c * 2048 + 1: c * 2048 + 17],
                      stk(c, 0), [("LTB", 0)])
            for c in range(8):
                transpose(PTB[0:16, (c % 4) * 128:(c % 4) * 128 + 128],
                          LTB[:, c * 16:(c + 1) * 16], IDB[:],
                          [("LTB", 0), ("IDB", 0)], [("PTB", 0)])
                em.op("V", lambda e, c_=c: e.tensor_scalar_mul(
                    LOW[0:16, c_ * 128:(c_ + 1) * 128],
                    PTB[0:16, (c_ % 4) * 128:(c_ % 4) * 128 + 128], SC16[:]),
                    r=[("PTB", 0), ("SC16", 0)], w=[("LOW", c // 4)])
            for r8 in range(8):
                rr = [0, 2, 4, 6, 8, 10, 12, 14][r8]
                base, cb = 32 * (r8 % 3), r8 // 3
                dma("Y", SGQK[base:base + 1, cb * 1024:(cb + 1) * 1024], LOW[rr:rr + 1, 0:1024],
                    [("LOW", 0), ("LOW", 1)], qkk(cb * 1024, (cb + 1) * 1024),
                    slot="LT%d" % r8)

            def ltm_ap(r8, lo, hi, step=1):
                base, cb = 32 * (r8 % 3), r8 // 3
                return SGQK[base:base + 1, cb * 1024 + lo: cb * 1024 + hi: step] \
                    if step != 1 else SGQK[base:base + 1, cb * 1024 + lo: cb * 1024 + hi]

            def ur_ap(i):
                base = 32 * (i % 3)
                cb = 0 if i < 3 else 1
                return SGQK[base:base + 1, 3072 + cb * 512: 3072 + cb * 512 + 512]

            def t2r_ap(j, i):
                base = 32 * ((4 + i) % 3)
                cb = j if i < 3 else 4 + j
                return SGKV[base:base + 1, cb * 512:(cb + 1) * 512]

            LTMK = qkk(0, 3072)
            URK = qkk(3072, 4096)
            T2K = kvk(0, 4096)

            for s in range(4):
                ss = s * 512
                for c in range(8):
                    vcopy(xs_ap(c), st_ap(c, ss, ss + 512), stk(c, s),
                          xsk(c * 512, (c + 1) * 512))
                for c in range(8):
                    mm(PTF[0:4, 0:512], LTB[:, c * 16 + 1: c * 16 + 8: 2], xs_ap(c),
                       c == 0, c == 7,
                       [("LTB", 0)] + xsk(c * 512, (c + 1) * 512), ptfk(0, 512))
                vcopy(SQT[0:4, 0:512], PTF[0:4, 0:512], ptfk(0, 512), [("SQT", 0)])
                for i in range(4):
                    dma("Y", ur_ap(i), SQT[i:i + 1, 0:512], [("SQT", 0)], URK,
                        slot="UR%d" % i)
                for oc in range(32):
                    reg = PSR[oc % 4]
                    slot = sm_slot[0] % 16
                    sm_slot[0] += 1
                    vb_ = slot * 1024
                    dma("G", VC[:, vb_:vb_ + 1024], w1t[oc],
                        (), vck(vb_, vb_ + 1024), slot="VW%d" % slot)
                    for c in range(8):
                        mm(psum_ap(reg), VC[:, vb_ + c * 128: vb_ + c * 128 + 128], xs_ap(c),
                           c == 0, False,
                           vck(vb_, vb_ + 1024) + xsk(c * 512, (c + 1) * 512), [reg])
                    i = oc // 8
                    mm(psum_ap(reg), ltm_ap(i, (oc % 8) * 128, (oc % 8) * 128 + 128),
                       ur_ap(i), False, True, LTMK + URK, [reg])
                    act(A1[:, oc * 512:(oc + 1) * 512], psum_ap(reg), AF.Gelu,
                        [reg, ("B1S", 0)], a1k(oc * 512, (oc + 1) * 512),
                        bias=B1S[:, oc:oc + 1])
                for j in range(4):
                    for c in range(8):
                        hc = j * 8 + c
                        mm(PTF[0:4, 0:512], LTB[:, c * 16 + 9: c * 16 + 16: 2],
                           A1[:, hc * 512:(hc + 1) * 512], c == 0, c == 7,
                           [("LTB", 0)] + a1k(hc * 512, (hc + 1) * 512), ptfk(0, 512))
                    vcopy(SQT[0:4, 0:512], PTF[0:4, 0:512], ptfk(0, 512), [("SQT", 0)])
                    for i in range(4):
                        dma("Y", t2r_ap(j, i), SQT[i:i + 1, 0:512], [("SQT", 0)], T2K,
                            slot="T2%d" % (j * 4 + i))
                for oc in range(8):
                    reg = PSR[4 + oc % 2]
                    for iq in range(4):
                        slot = sm_slot[0] % 16
                        sm_slot[0] += 1
                        vb_ = slot * 1024
                        dma("G", VC[:, vb_:vb_ + 1024], w2t[oc * 4 + iq],
                            (), vck(vb_, vb_ + 1024), slot="VW%d" % slot)
                        for c8 in range(8):
                            c = iq * 8 + c8
                            mm(psum_ap(reg), VC[:, vb_ + c8 * 128: vb_ + c8 * 128 + 128],
                               A1[:, c * 512:(c + 1) * 512], c == 0, False,
                               vck(vb_, vb_ + 1024) + a1k(c * 512, (c + 1) * 512), [reg])
                    i = oc // 2
                    for j in range(4):
                        lh = ltm_ap(4 + i, j + 4 * ((oc % 2) * 128),
                                    j + 4 * ((oc % 2) * 128) + 512, step=4)
                        mm(psum_ap(reg), lh, t2r_ap(j, i), False, j == 3,
                           LTMK + T2K, [reg])
                    stt(st_ap(oc, ss, ss + 512), psum_ap(reg), B2S[:, oc:oc + 1],
                        st_ap(oc, ss, ss + 512), ALU.add, ALU.add,
                        [reg, ("B2S", 0)] + stk(oc, s), stk(oc, s))
                layer_norm(s, "x2")

        def build_p3(q):
            sm_slot = [0]
            QSB, PSL = 0, 4096

            for s in range(4):
                ss = s * 512
                for c in range(8):
                    vcopy(xs_ap(c), st_ap(c, ss, ss + 512), stk(c, s),
                          xsk(c * 512, (c + 1) * 512))
                for oc in range(24):
                    reg = PSR[2 + oc % 2]
                    s14 = sm_slot[0] % 14
                    sm_slot[0] += 1
                    if s14 < 4:
                        stage, skeys, snm = (SGSM[:, s14 * 1024:(s14 + 1) * 1024],
                                             [("SM", s14)], "SM%d" % s14)
                    elif s14 < 6:
                        xb_ = 6144 + (s14 - 4) * 1024
                        stage, skeys, snm = (XH[:, xb_:xb_ + 1024],
                                             xhk(xb_, xb_ + 1024), "QX%d" % (s14 - 4))
                    elif s14 < 7:
                        qb3 = 3072
                        stage, skeys, snm = (SGSM[:, qb3:qb3 + 1024],
                                             [("SM", 3)], "SM3")
                    elif s14 < 11:
                        kb2 = (s14 - 7) * 1024
                        stage, skeys, snm = (SGKV[:, kb2:kb2 + 1024],
                                             kvk(kb2, kb2 + 1024), "KS%d" % (s14 - 7))
                    else:
                        qb2 = (s14 - 10) * 1024
                        stage, skeys, snm = (SGQK[:, qb2:qb2 + 1024],
                                             qkk(qb2, qb2 + 1024), "QS%d" % (s14 - 10))
                    dma("G", stage, wqkv[oc], (), skeys, slot=snm)
                    for c in range(8):
                        mm(psum_ap(reg), stage[:, c * 128:(c + 1) * 128], xs_ap(c),
                           c == 0, c == 7,
                           skeys + xsk(c * 512, (c + 1) * 512), [reg])
                    if oc < 8:
                        xb = QSB + oc * 512
                        act(XH[:, xb:xb + 512], psum_ap(reg), AF.Identity,
                            [reg, ("BQK", 0), ("CVS", 0)], xhk(xb, xb + 512),
                            bias=BQK[:, oc:oc + 1], scale=CVS[:, 5:6])
                    elif oc < 16:
                        ab = (oc - 8) * 2048 + ss
                        act(A1[:, ab:ab + 512], psum_ap(reg), AF.Identity,
                            [reg, ("BQK", 0)], a1k(ab, ab + 512), bias=BQK[:, oc:oc + 1])
                    else:
                        vslot = (oc - 16) % 2
                        act(SGQK[:, vslot * 512: vslot * 512 + 512], psum_ap(reg),
                            AF.Identity, [reg, ("BQK", 0)], qkk(vslot * 512, vslot * 512 + 512),
                            bias=BQK[:, oc:oc + 1])
                        for t4 in range(4):
                            transpose(PTB[:, t4 * 128:(t4 + 1) * 128],
                                      SGQK[:, vslot * 512 + t4 * 128: vslot * 512 + t4 * 128 + 128],
                                      IDB[:], qkk(vslot * 512, vslot * 512 + 512) + [("IDB", 0)],
                                      [("PTB", 0)])
                        for t4 in range(4):
                            tb = s * 4 + t4
                            for hh in range(2):
                                h_ = (oc - 16) * 2 + hh
                                vb = (tb * 16 + h_) * 65
                                vcopy(VC[:, vb:vb + 64],
                                      PTB[:, t4 * 128 + hh * 64: t4 * 128 + hh * 64 + 64],
                                      [("PTB", 0)], vck(vb, vb + 64))
                vb0 = s * 4 * 16 * 65
                em.op("V", lambda e, b=vb0: e.tensor_copy(
                    VC[:, b + 64: b + 64 + 63 * 65 + 1: 65], ONVB[:]),
                    r=[("ONVB", 0)], w=vck(vb0, vb0 + 64 * 65))
                for h in range(16):
                    if h % 2 == 0:
                        yps, yk = PTF, ptfk(0, 512)
                    else:
                        yps, yk = PB[:, 0:512], [("PB", 0)]
                    nkb = 4 * s + 4
                    npair = nkb // 2
                    qk_ = xhk(QSB + (h // 2) * 512, QSB + (h // 2) * 512 + 512)

                    def pair_psum(k, lo, hi):
                        if k % 2 == 0:
                            return PA[:, lo:hi], [("PA", 0), ("PA", 1)]
                        return PB[:, 512 + lo: 512 + hi], [("PB", 1), ("PB", 2)]

                    def sc_pair(k, h=h, qk_=qk_):
                        c0min = 128 * max(2 * k - 4 * s, 0)
                        for o2 in (0, 1):
                            kb = 2 * k + o2
                            d = kb - 4 * s
                            col0 = 128 * max(d, 0)
                            kb_lo = (h // 2) * 2048 + kb * 128
                            kap = A1[(h % 2) * 64:(h % 2) * 64 + 64, kb_lo:kb_lo + 128]
                            qap = XH[(h % 2) * 64:(h % 2) * 64 + 64,
                                     QSB + (h // 2) * 512 + col0: QSB + (h // 2) * 512 + 512]
                            pp, ppk = pair_psum(k, o2 * 512 + col0, o2 * 512 + 512)
                            mm(pp, kap, qap, True, True,
                               a1k(kb_lo, kb_lo + 128) + qk_, ppk)
                        pb = PSL + (k % 2) * 1024
                        pp, ppk = pair_psum(k, c0min, 1024)
                        act(XH[:, pb + c0min: pb + 1024], pp, AF.Exp,
                            ppk, xhk(pb, pb + 1024))
                        for o2 in (0, 1):
                            kb = 2 * k + o2
                            d = kb - 4 * s
                            if d >= 0:
                                col0 = 128 * d
                                b0 = pb + o2 * 512 + col0
                                tt(XH[:, b0:b0 + 128], XH[:, b0:b0 + 128], TRB[:],
                                   ALU.mult, xhk(pb, pb + 1024) + [("TRB", 0)],
                                   xhk(pb, pb + 1024), eng="G")

                    def av_pair(k, h=h, yps=yps, yk=yk, nkb=nkb):
                        pb = PSL + (k % 2) * 1024
                        for o2 in (0, 1):
                            kb = 2 * k + o2
                            d = kb - 4 * s
                            col0 = 128 * max(d, 0)
                            mm(yps[0:65, col0:512],
                               VC[:, (kb * 16 + h) * 65:(kb * 16 + h) * 65 + 65],
                               XH[:, pb + o2 * 512 + col0: pb + o2 * 512 + 512],
                               kb == 0, kb == nkb - 1,
                               vck((kb * 16 + h) * 65, (kb * 16 + h) * 65 + 65) +
                               xhk(pb, pb + 1024), yk)

                    sc_pair(0)
                    for k in range(npair):
                        if k + 1 < npair:
                            sc_pair(k + 1)
                        av_pair(k)
                    vcopy(SROW[0:1, 0:512], yps[64:65, 0:512], yk, [("SROW", 0)])
                    act(SROW[0:1, 0:512], SROW[0:1, 0:512], AF.Ln,
                        [("SROW", 0)], [("SROW", 0)])
                    act(SROW[0:1, 0:512], SROW[0:1, 0:512], AF.Exp,
                        [("SROW", 0)], [("SROW", 0)], scale=-1.0)
                    mm(PA[:, 1024:1536][0:64, :], ON1F[0:1, 0:64], SROW[0:1, 0:512],
                       True, True, [("SROW", 0), ("ON1F", 0)], [("PA", 2)])
                    vcopy(AFS[0:64, C_RB:C_RB + 512], PA[0:64, 1024:1536],
                          [("PA", 2)], afk(C_RB, C_RB + 512))
                    tt(AFS[0:64, C_YT:C_YT + 512], yps[0:64, 0:512],
                       AFS[0:64, C_RB:C_RB + 512], ALU.mult,
                       yk + afk(C_RB, C_RB + 512), afk(C_YT, C_YT + 512))
                    YTP = 1024
                    if h % 2 == 0:
                        vcopy(AFS[0:64, YTP:YTP + 512], AFS[0:64, C_YT:C_YT + 512],
                              afk(C_YT, C_YT + 512), afk(YTP, YTP + 512))
                    else:
                        vcopy(AFS[64:128, YTP:YTP + 512], AFS[0:64, C_YT:C_YT + 512],
                              afk(C_YT, C_YT + 512), afk(YTP, YTP + 512))
                        c = h // 2
                        tt(st_ap(c, ss, ss + 512), AFS[:, YTP:YTP + 512],
                           st_ap(c, ss, ss + 512), ALU.add,
                           afk(YTP, YTP + 512) + stk(c, s), stk(c, s))
                layer_norm(s, "x3")

        def build_p4(q):
            sm_slot = [0]
            XR, XK, KA, RS = 0, 4096, 8192, 12288
            for s in range(4):
                ss = s * 512
                for c in range(8):
                    for (dst, mu, om, omk_) in ((XR, MURS, OMR, "OMR"), (XK, MUKS, OMK, "OMK")):
                        tsmul(afa(C_MX, C_MX + 512), st_ap(c, ss, ss + 512), mu[:, c:c + 1],
                              stk(c, s) + [("MUR", 0), ("MUK", 0)], afk(C_MX, C_MX + 512))
                        base = dst + c * 512
                        if s == 0:
                            vcopy(VC[:, base:base + 1], afa(C_MX, C_MX + 1),
                                  afk(C_MX, C_MX + 512), vck(base, base + 512))
                            stt(VC[:, base + 1:base + 512], st_ap(c, 0, 511),
                                om[:, c:c + 1], afa(C_MX + 1, C_MX + 512), ALU.mult, ALU.add,
                                stk(c, s) + [(omk_, 0)] + afk(C_MX, C_MX + 512),
                                vck(base, base + 512))
                        else:
                            stt(VC[:, base:base + 512], st_ap(c, ss - 1, ss + 511),
                                om[:, c:c + 1], afa(C_MX, C_MX + 512), ALU.mult, ALU.add,
                                stk(c, s) + stk(c, s - 1) + [(omk_, 0)] + afk(C_MX, C_MX + 512),
                                vck(base, base + 512))
                for m, srcb, dstb in ((0, XR, RS), (1, XK, KA), (2, KA, None)):
                    for oc in range(8):
                        reg = PSR[oc % 4]
                        s20 = sm_slot[0] % 20
                        sm_slot[0] += 1
                        if s20 < 12:
                            stage, skeys, snm = (A1[:, s20 * 1024:(s20 + 1) * 1024],
                                                 a1k(s20 * 1024, (s20 + 1) * 1024),
                                                 "RK%d" % s20)
                        elif s20 < 16:
                            kb2 = (s20 - 12) * 1024
                            stage, skeys, snm = (SGKV[:, kb2:kb2 + 1024],
                                                 kvk(kb2, kb2 + 1024), "KS%d" % (s20 - 12))
                        else:
                            qb2 = (s20 - 16) * 1024
                            stage, skeys, snm = (SGQK[:, qb2:qb2 + 1024],
                                                 qkk(qb2, qb2 + 1024), "QS%d" % (s20 - 16))
                        dma("G", stage, wrkv[m, oc], (), skeys, slot=snm)
                        for c in range(8):
                            sb_ = srcb + c * 512
                            mm(psum_ap(reg), stage[:, c * 128:(c + 1) * 128],
                               VC[:, sb_:sb_ + 512], c == 0, c == 7,
                               skeys + vck(sb_, sb_ + 512), [reg])
                        if m == 0:
                            act(afa(C_S1, C_S1 + 512), psum_ap(reg), AF.Exp,
                                [reg], afk(C_S1, C_S1 + 512), scale=-1.0)
                            act(afa(C_S1, C_S1 + 512), afa(C_S1, C_S1 + 512), AF.Ln,
                                afk(C_S1, C_S1 + 512), afk(C_S1, C_S1 + 512), bias=1.0)
                            db = RS + oc * 512
                            act(VC[:, db:db + 512], afa(C_S1, C_S1 + 512), AF.Exp,
                                afk(C_S1, C_S1 + 512), vck(db, db + 512), scale=-1.0)
                        elif m == 1:
                            act(afa(C_S1, C_S1 + 512), psum_ap(reg), AF.Relu,
                                [reg], afk(C_S1, C_S1 + 512))
                            db = KA + oc * 512
                            act(VC[:, db:db + 512], afa(C_S1, C_S1 + 512), AF.Square,
                                afk(C_S1, C_S1 + 512), vck(db, db + 512))
                        else:
                            rb_ = RS + oc * 512
                            tt(afa(C_S1, C_S1 + 512), psum_ap(reg), VC[:, rb_:rb_ + 512],
                               ALU.mult, [reg] + vck(rb_, rb_ + 512), afk(C_S1, C_S1 + 512))
                            tt(st_ap(oc, ss, ss + 512), afa(C_S1, C_S1 + 512),
                               st_ap(oc, ss, ss + 512), ALU.add,
                               afk(C_S1, C_S1 + 512) + stk(oc, s), stk(oc, s))
                layer_norm(s, "x4")
                for t4 in range(4):
                    tb = s * 4 + t4
                    ob = C_X0R
                    for cg in range(2):
                        for c4 in range(4):
                            c = cg * 4 + c4
                            transpose(PTF[:, c4 * 128:(c4 + 1) * 128],
                                      st_ap(c, ss + t4 * 128, ss + t4 * 128 + 128), IDF[:],
                                      stk(c, s) + [("IDF", 0)], ptfk(c4 * 128, (c4 + 1) * 128))
                        vcopy(afa(ob + cg * 512, ob + cg * 512 + 512), PTF[:, 0:512],
                              ptfk(0, 512), afk(ob + cg * 512, ob + cg * 512 + 512))
                    out_dmas.append(
                        dma("Y", out_d[q, tb], afa(ob, ob + 1024), afk(ob, ob + 1024), ()))

        for q in range(min(NSEQ, MAX_SEQ)):
            cur_q[0] = q
            build_p1(q)
            if MAX_PHASE >= 2:
                build_p2(q)
            if MAX_PHASE >= 3:
                build_p3(q)
            if MAX_PHASE >= 4:
                build_p4(q)

        em.op("V", lambda e: e.nop(), deps=out_dmas)

        em.paranoid = bool(PARANOID)
        em.finalize(paranoid=bool(PARANOID))
        sem = {n: es.enter_context(nc.semaphore("sm" + n)) for n in em.sem_names}
        if os.environ.get("K_DUMP"):
            with open("opdump.txt", "w") as f:
                for o in em.ops:
                    f.write(f"{o.idx} {o.eng} {o.kind} waits={o.waits} "
                            f"inc={o.inc_sem}:{o.inc_val} r={o.r} w={o.w}\n")
        streams = {e: [o for o in em.ops if o.eng == e] for e in ("T", "S", "V", "G", "Y")}

        with nc.Block() as blk:
            def emit_stream(ops):
                def run(e):
                    for o in ops:
                        for (s_, v_) in o.waits:
                            e.wait_ge(sem[s_], v_)
                        inst = o.fn(e)
                        if o.inc_sem is not None:
                            inst.then_inc(sem[o.inc_sem], 16 if o.kind == "dma" else 1)
                return run

            blk.tensor(emit_stream(streams["T"]))
            blk.scalar(emit_stream(streams["S"]))
            blk.vector(emit_stream(streams["V"]))
            blk.gpsimd(emit_stream(streams["G"]))
            blk.sync(emit_stream(streams["Y"]))

    return nc


_CACHE = {}


def _host_prep(inputs):
    f32 = np.float32
    ca = np.ascontiguousarray
    Wq = inputs["L0_attn_Wq"]; Wk = inputs["L0_attn_Wk"]
    Wv = inputs["L0_attn_Wv"]; Wg = inputs["L0_attn_Wg"]
    d = {}
    wqk = np.empty((NH, 8, 128, 512), f32)
    for h in range(NH):
        for c in range(8):
            wqk[h, c, :, 0:256] = Wq[c * 128:(c + 1) * 128, h * 256:(h + 1) * 256]
            wqk[h, c, :, 256:512] = Wk[c * 128:(c + 1) * 128, h * 256:(h + 1) * 256]
    d["wqk"] = ca(wqk.transpose(0, 2, 1, 3).reshape(NH, 128, 4096))
    d["wktm"] = ca(Wk.reshape(8, 128, NH, 256).transpose(2, 1, 0, 3).reshape(NH, 128, 2048))
    d["wvtm"] = ca(Wv.reshape(8, 128, NH, 512).transpose(2, 1, 0, 3).reshape(NH, 128, 4096))
    d["wgtm"] = ca(Wg.reshape(8, 128, NH, 512).transpose(2, 1, 0, 3).reshape(NH, 128, 4096))
    d["wgk1"] = ca(inputs["L0_attn_Wgk1"].reshape(8, 128, 16).transpose(1, 0, 2).reshape(128, 128))
    wgk2b = np.empty((NH, 17, 256), f32)
    for h in range(NH):
        wgk2b[h, 0:16] = inputs["L0_attn_Wgk2"][:, h * 256:(h + 1) * 256]
        wgk2b[h, 16] = inputs["L0_attn_bgk2"][h * 256:(h + 1) * 256]
    d["wgk2b"] = ca(wgk2b.transpose(1, 0, 2).reshape(17, 1024))
    d["gnw"] = ca(inputs["L0_attn_g_norm_w"].reshape(1, 512))
    wof4 = inputs["L0_attn_Wo"].reshape(2, 8, 128, 8, 128)
    d["wof"] = ca(wof4.transpose(3, 0, 2, 1, 4).reshape(16, 128, 1024))
    d["w1t"] = ca(inputs["L0_ffn_W1"].reshape(32, 128, 8, 128).transpose(0, 2, 3, 1)
                  .transpose(0, 2, 1, 3).reshape(32, 128, 1024))
    d["b1c"] = ca(inputs["L0_ffn_b1"].reshape(32, 128).T)
    w2p = inputs["L0_ffn_W2"].reshape(8, 128, 32, 128).transpose(0, 2, 3, 1)
    d["w2t"] = ca(w2p.reshape(8, 4, 8, 128, 128).transpose(0, 1, 3, 2, 4).reshape(32, 128, 1024))
    d["b2c"] = ca(inputs["L0_ffn_b2"].reshape(8, 128).T)
    qkvp = inputs["L1_attn_W"].reshape(24, 128, 8, 128).transpose(0, 2, 3, 1)
    d["wqkv"] = ca(qkvp.transpose(0, 2, 1, 3).reshape(24, 128, 1024))
    d["bqkv"] = ca(inputs["L1_attn_b"].reshape(24, 128).T)
    wrkv = np.stack([inputs["L1_ffn_W_r"], inputs["L1_ffn_W_k"], inputs["L1_ffn_W_v"]])
    rkvp = wrkv.reshape(3, 8, 128, 8, 128).transpose(0, 1, 3, 4, 2)
    d["wrkv"] = ca(rkvp.transpose(0, 1, 3, 2, 4).reshape(3, 8, 128, 1024))
    d["mur"] = ca(inputs["L1_ffn_mu_r"].reshape(8, 128).T)
    d["muk"] = ca(inputs["L1_ffn_mu_k"].reshape(8, 128).T)
    d["scl"] = np.broadcast_to(np.asarray(inputs["L0_ffn_scale"], f32).reshape(1, 1),
                               (16, 1)).copy()
    ident = np.eye(128, dtype=f32)
    triu = np.triu(np.ones((128, 128), f32))
    mstrict = np.tril(np.ones((128, 128), f32), -1)
    d["cst"] = ca(np.concatenate([ident, triu, mstrict], axis=1))
    d["onesd"] = np.ones((128, 1024), f32)
    cv = np.zeros((128, 8), f32)
    cv[:, 0] = 512 * 1e-5
    cv[:, 1] = 0.5 * math.log(512.0)
    cv[:, 2] = 1e-5
    cv[:, 3] = -0.5
    cv[:, 4] = 1.0 / 16.0
    cv[:, 5] = 0.125
    d["cvals"] = cv
    return d


def kernel(**inputs):
    if "nc" not in _CACHE:
        _CACHE["nc"] = build_nc()
    nc = _CACHE["nc"]
    shared = _host_prep(inputs)
    x = inputs["L0_attn_hidden_states"]
    in_maps = []
    for c in range(8):
        m = dict(shared)
        m["x0"] = np.ascontiguousarray(
            x[2 * c: 2 * c + 2].transpose(0, 2, 1)).reshape(NSEQ, 8, 128, T)
        in_maps.append(m)
    res = run_bass_kernel_spmd(nc, in_maps, core_ids=list(range(8)))
    outs = [res.results[c]["out"].reshape(T * NSEQ, 1024) for c in range(8)]
    full = np.concatenate(outs, axis=0).reshape(16, T, 1024)
    return np.ascontiguousarray(full.astype(np.float32))


if __name__ == "__main__":
    nc = build_nc()
    print("built ok")
